# version 14
# baseline (speedup 1.0000x reference)
import numpy as np
import ml_dtypes
from contextlib import ExitStack

import concourse.bass as bass
import concourse.mybir as mybir
from concourse.bass_utils import run_bass_kernel_spmd

F32 = mybir.dt.float32
BF16 = mybir.dt.bfloat16
AF = mybir.ActivationFunctionType
ALU = mybir.AluOpType
AX = mybir.AxisListType

T = 4096
D = 2048
KC = 16
H = 16
DH = 128
BLK = 256
SUB = 512
NSUB = T // SUB
FQKV = 48
NE = 8
HEXP = 1408
HJ = 11
G = 1024
NG = T // G
NPASS = 10

ENGS = ("t", "v", "s", "g", "y")
SEMS = {}


class Plan:
    def __init__(self):
        self.ops = {e: [] for e in ENGS}
        self.cnt = {e: 0 for e in ENGS}
        self.lastwait = {}
        self.log = {e: [] for e in ENGS}

    def emit(self, eng, fn, inc=True):
        if inc:
            step = 16 if eng in ("g", "y") else 1
            self.cnt[eng] += step
            val = self.cnt[eng]

            def wrapped(e, fn=fn, eng=eng, step=step):
                fn(e).then_inc(SEMS[eng], step)

            self.ops[eng].append(wrapped)
            self.log[eng].append(("inc", step))
            return val
        self.ops[eng].append(fn)
        self.log[eng].append(("op", 0))
        return None

    def wait(self, eng, dep, val):
        if val is None or val <= 0:
            return
        key = (eng, dep)
        if self.lastwait.get(key, 0) >= val:
            return
        self.lastwait[key] = val
        self.ops[eng].append(lambda e, d=dep, v=val: e.wait_ge(SEMS[d], v))
        self.log[eng].append(("wait", dep, val))

    def drain(self, eng):
        self.ops[eng].append(lambda e: e.drain())
        self.log[eng].append(("op", 0))

    def barrier(self):
        marks = {}
        for e in ENGS:
            step = 16 if e in ("g", "y") else 1
            if e in ("g", "y"):
                self.wait(e, e, self.cnt[e])
            else:
                self.drain(e)
            self.cnt[e] += step
            v = self.cnt[e]

            def mk(e=e, step=step):
                def f(en):
                    en.nop().then_inc(SEMS[e], step)
                return f
            self.ops[e].append(mk())
            self.log[e].append(("inc", step))
            marks[e] = v
        for e in ENGS:
            for d in ENGS:
                if d != e:
                    self.wait(e, d, marks[d])
        return marks


def build_program(return_plan=False):
    P = Plan()
    nc = bass.Bass("TRN2", target_bir_lowering=False)

    x_h = nc.dram_tensor("x", [T, D], F32, kind="ExternalInput").ap()
    wqkv_h = nc.dram_tensor("w_qkv", [3 * D, D], F32, kind="ExternalInput").ap()
    wgate_h = nc.dram_tensor("w_gate", [D, D], F32, kind="ExternalInput").ap()
    wout_h = nc.dram_tensor("w_out", [D, D], F32, kind="ExternalInput").ap()
    slope_h = nc.dram_tensor("slope", [1, H], F32, kind="ExternalInput").ap()
    rw_h = nc.dram_tensor("router_w", [NE, D], F32, kind="ExternalInput").ap()
    wg_h = nc.dram_tensor("wg", [NE, D, HEXP], F32, kind="ExternalInput").ap()
    wu_h = nc.dram_tensor("wu", [NE, D, HEXP], F32, kind="ExternalInput").ap()
    wd_h = nc.dram_tensor("wd", [NE, HEXP, D], F32, kind="ExternalInput").ap()
    sg_h = nc.dram_tensor("sg", [D, 2 * HEXP], F32, kind="ExternalInput").ap()
    su_h = nc.dram_tensor("su", [D, 2 * HEXP], F32, kind="ExternalInput").ap()
    sd_h = nc.dram_tensor("sd", [2 * HEXP, D], F32, kind="ExternalInput").ap()
    c_id_bf = nc.dram_tensor("c_id_bf", [128, 128], BF16, kind="ExternalInput").ap()
    c_id_f32 = nc.dram_tensor("c_id_f32", [128, 128], F32, kind="ExternalInput").ap()
    c_ones_f32 = nc.dram_tensor("c_ones_f32", [128, 128], F32, kind="ExternalInput").ap()
    c_ones_bf = nc.dram_tensor("c_ones_bf", [128, 1], BF16, kind="ExternalInput").ap()
    c_ar = nc.dram_tensor("c_ar", [128, BLK], F32, kind="ExternalInput").ap()
    c_revar = nc.dram_tensor("c_revar", [128, 2], F32, kind="ExternalInput").ap()
    c_idx = nc.dram_tensor("c_idx", [128, 2 * BLK], F32, kind="ExternalInput").ap()
    c_misc = nc.dram_tensor("c_misc", [128, 8], F32, kind="ExternalInput").ap()
    c_normw_t = nc.dram_tensor("c_normw_t", [128, KC], F32, kind="ExternalInput").ap()

    out_h = nc.dram_tensor("out", [T, D], F32, kind="ExternalOutput").ap()

    atc_h = nc.dram_tensor("attn_cache", [80, 128, D], BF16, kind="Internal").ap()
    wdc_h = nc.dram_tensor("wd_cache", [NPASS, KC, 128, HEXP], BF16, kind="Internal").ap()
    xnc_h = nc.dram_tensor("xnT_cache", [D, T], BF16, kind="Internal").ap()
    ysc_h = nc.dram_tensor("attnY_scratch", [D, T], F32, kind="Internal").ap()

    with ExitStack() as SG:
        eg = SG.enter_context
        for name in ENGS:
            SEMS[name] = eg(nc.semaphore("sem_" + name))

        id_bf = eg(nc.sbuf_tensor("id_bf", [128, 128], BF16))
        id_f32 = eg(nc.sbuf_tensor("id_f32", [128, 128], F32))
        ones_f32 = eg(nc.sbuf_tensor("ones_f32", [128, 128], F32))
        ones_bf = eg(nc.sbuf_tensor("ones_bf", [128, 1], BF16))
        misc = eg(nc.sbuf_tensor("misc", [128, 8], F32))
        normw_sb = eg(nc.sbuf_tensor("normw_sb", [128, KC], F32))
        rwT = eg(nc.sbuf_tensor("rwT", [128, KC * NE], BF16))

        cl = None
        for dst, src in ((id_bf, c_id_bf), (id_f32, c_id_f32), (ones_f32, c_ones_f32),
                         (ones_bf, c_ones_bf), (misc, c_misc), (normw_sb, c_normw_t)):
            cl = P.emit("y", lambda e, d=dst, s=src: e.dma_start(d[:], s[:, :]))

        with ExitStack() as SA:
            ea = SA.enter_context
            qdecay = ea(nc.sbuf_tensor("qdecay", [128, H * BLK], BF16))
            diag = ea(nc.sbuf_tensor("diag", [128, H * 2 * BLK], BF16))
            kdecay = ea(nc.sbuf_tensor("kdecay", [128, 2 * H], F32))
            bd_sb = ea(nc.sbuf_tensor("bd_sb", [128, H], F32))
            kv_sb = ea(nc.sbuf_tensor("kv_sb", [128, H * DH], F32))
            kvbf = ea(nc.sbuf_tensor("kvbf", [128, H * DH], BF16))

            with ExitStack() as SP:
                ep = SP.enter_context
                slope_sb = ep(nc.sbuf_tensor("slope_sb", [1, H], F32))
                srep = ep(nc.sbuf_tensor("srep", [128, H], F32))
                car_sb = ep(nc.sbuf_tensor("car_sb", [128, BLK], F32))
                crev_sb = ep(nc.sbuf_tensor("crev_sb", [128, 2], F32))
                cidx_sb = ep(nc.sbuf_tensor("cidx_sb", [128, 2 * BLK], F32))
                wstage = [ep(nc.sbuf_tensor(f"wstage{i}", [128, D], BF16)) for i in range(2)]
                wtout = [ep(nc.sbuf_tensor(f"wtout{i}", [128, D], BF16)) for i in range(2)]
                rwstage = ep(nc.sbuf_tensor("rwstage", [8, D], BF16))
                tq = ep(nc.sbuf_tensor("tq", [128, 2 * BLK], F32))
                td = ep(nc.sbuf_tensor("td", [128, 4 * BLK], F32))
                tk = ep(nc.sbuf_tensor("tk", [128, 4], F32))
                p_pro = ep(nc.psum_tensor("p_pro", [128, 2 * D], BF16))
                p_sr = ep(nc.psum_tensor("p_sr", [128, 512], F32))

                cl2 = None
                for dst, srcap in ((slope_sb, slope_h), (car_sb, c_ar), (crev_sb, c_revar),
                                   (cidx_sb, c_idx)):
                    cl2 = P.emit("y", lambda e, d=dst, s=srcap: e.dma_start(d[:], s[:, :]))

                P.wait("t", "y", cl)
                P.wait("t", "y", cl2)
                pe_sr = P.emit("t", lambda e: e.matmul(
                    p_sr[:, 0:H], ones_f32[0:1, :], slope_sb[0:1, :], start=True, stop=True))
                P.wait("v", "t", pe_sr)
                dv_sr = P.emit("v", lambda e: e.tensor_copy(srep[:], p_sr[:, 0:H]))
                P.drain("v")

                for h in range(H):
                    par = h % 2
                    vq = P.emit("v", lambda e, h=h, par=par: e.tensor_scalar(
                        tq[:, par * BLK:(par + 1) * BLK], car_sb[:],
                        srep[:, h:h + 1], None, ALU.mult))
                    vd = P.emit("v", lambda e, h=h, par=par: e.tensor_scalar(
                        td[:, par * 2 * BLK:(par + 1) * 2 * BLK], cidx_sb[:],
                        srep[:, h:h + 1], None, ALU.mult))
                    vk = P.emit("v", lambda e, h=h, par=par: e.tensor_scalar(
                        tk[:, par * 2:(par + 1) * 2], crev_sb[:],
                        srep[:, h:h + 1], None, ALU.mult))
                    P.wait("s", "v", vk)
                    P.emit("s", lambda e, h=h, par=par: e.activation(
                        qdecay[:, h * BLK:(h + 1) * BLK], tq[:, par * BLK:(par + 1) * BLK],
                        AF.Exp, scale=-1.0))
                    P.emit("s", lambda e, h=h, par=par: e.activation(
                        diag[:, h * 2 * BLK:(h + 1) * 2 * BLK],
                        td[:, par * 2 * BLK:(par + 1) * 2 * BLK], AF.Exp, scale=-1.0))
                    sk = P.emit("s", lambda e, h=h, par=par: e.activation(
                        kdecay[:, 2 * h:2 * h + 2], tk[:, par * 2:(par + 1) * 2],
                        AF.Exp, scale=-1.0))
                    P.wait("v", "s", sk)
                bdv = P.emit("s", lambda e: e.activation(bd_sb[:], srep[:], AF.Exp, scale=-256.0))

                gl = P.emit("g", lambda e: e.dma_start(rwstage[:], rw_h[:, :]))
                P.wait("t", "g", gl)
                pe_rw = None
                for k in range(KC):
                    pe_rw = P.emit("t", lambda e, k=k: e.transpose(
                        p_pro[0:128, k * 8:(k + 1) * 8], rwstage[0:8, k * 128:(k + 1) * 128],
                        id_bf[0:8, 0:8]))
                P.wait("v", "t", pe_rw)
                rwT_ev = P.emit("v", lambda e: e.tensor_copy(rwT[:], p_pro[:, 0:KC * NE]))

                src_list = []
                for f in range(FQKV):
                    src_list.append(wqkv_h[f * 128:(f + 1) * 128, :])
                for f in range(KC):
                    src_list.append(wgate_h[f * 128:(f + 1) * 128, :])
                for f in range(KC):
                    src_list.append(wout_h[f * 128:(f + 1) * 128, :])
                ld_done = [None] * 80
                tr_done = [None] * 80
                ev_done = [None] * 80
                st_done = [None] * 80
                for i in range(80):
                    par = i % 2
                    if i >= 2:
                        P.wait("g", "t", tr_done[i - 2])
                    ld_done[i] = P.emit("g", lambda e, i=i, par=par: e.dma_start(
                        wstage[par][:], src_list[i]))
                    P.wait("t", "g", ld_done[i])
                    if i >= 2:
                        P.wait("t", "v", ev_done[i - 2])
                    else:
                        P.wait("t", "v", rwT_ev)
                    for k in range(KC):
                        tr_done[i] = P.emit("t", lambda e, i=i, par=par, k=k: e.transpose(
                            p_pro[:, par * D + k * 128:par * D + (k + 1) * 128],
                            wstage[par][:, k * 128:(k + 1) * 128], id_bf[:]))
                    P.wait("v", "t", tr_done[i])
                    if i >= 2:
                        P.wait("v", "y", st_done[i - 2])
                    ev_done[i] = P.emit("v", lambda e, i=i, par=par: e.tensor_copy(
                        wtout[par][:], p_pro[:, par * D:(par + 1) * D]))
                    P.wait("y", "v", ev_done[i])
                    st_done[i] = P.emit("y", lambda e, i=i, par=par: e.dma_start(
                        atc_h[i, :, :], wtout[par][:]))


            P.barrier()

            x_sb = [ea(nc.sbuf_tensor(f"x_sb{i}", [128, D], F32)) for i in range(2)]
            lnb = ea(nc.sbuf_tensor("lnb", [128, 16], F32))
            xjunk = ea(nc.sbuf_tensor("xjunk", [128, D], BF16))
            xn_sb = [ea(nc.sbuf_tensor(f"xn_sb{i}", [128, D], BF16)) for i in range(4)]
            xnT = ea(nc.sbuf_tensor("xnT", [128, KC * SUB], BF16))
            qkvT = ea(nc.sbuf_tensor("qkvT", [128, FQKV * SUB], BF16))
            attnT = ea(nc.sbuf_tensor("attnT", [128, KC * SUB], BF16))
            gated = ea(nc.sbuf_tensor("gated", [128, KC * SUB], BF16))
            wblk = [ea(nc.sbuf_tensor(f"wblk{i}", [128, D], BF16)) for i in range(3)]
            qdec = [ea(nc.sbuf_tensor(f"qdec{i}", [128, BLK], BF16)) for i in range(2)]
            ktm = [ea(nc.sbuf_tensor(f"ktm{i}", [128, BLK], BF16)) for i in range(2)]
            kdtm = [ea(nc.sbuf_tensor(f"kdtm{i}", [128, BLK], BF16)) for i in range(2)]
            vtm = [ea(nc.sbuf_tensor(f"vtm{i}", [128, BLK], BF16)) for i in range(2)]
            qkd = [ea(nc.sbuf_tensor(f"qkd{i}", [128, SUB], BF16)) for i in range(2)]
            sq = [ea(nc.sbuf_tensor(f"sq{i}", [128, SUB], BF16)) for i in range(2)]
            lnrow = ea(nc.sbuf_tensor("lnrow", [1, 2 * SUB], F32))
            gate = [ea(nc.sbuf_tensor(f"gate{i}", [128, SUB], BF16)) for i in range(2)]
            gtmp = [ea(nc.sbuf_tensor(f"gtmp{i}", [128, SUB], BF16)) for i in range(2)]
            yevac = [ea(nc.sbuf_tensor(f"yevac{i}", [128, SUB], F32)) for i in range(2)]

            pa = ea(nc.psum_tensor("pa", [128, 3 * SUB], F32))
            pattn = ea(nc.psum_tensor("pattn", [128, 2 * SUB], F32))
            pkvu = ea(nc.psum_tensor("pkvu", [128, 2 * 128], F32))
            pmisc = ea(nc.psum_tensor("pmisc", [128, SUB], F32))
            ptr = ea(nc.psum_tensor("ptr", [128, SUB * 2], BF16))

            wd_items = []
            for p10 in range(NPASS):
                for m in range(KC):
                    if p10 < NE:
                        wsrc = wd_h[p10, :, m * 128:(m + 1) * 128].rearrange(
                            "(j p) c -> p j c", p=128)
                    else:
                        half = p10 - NE
                        wsrc = sd_h[half * HEXP:(half + 1) * HEXP,
                                    m * 128:(m + 1) * 128].rearrange(
                            "(j p) c -> p j c", p=128)
                    wd_items.append((wsrc, p10, m))
            wd_idx = 0
            wd_ld = {}
            wd_st = {}

            x_ld = [None, None]
            xn_done = [None, None, None, None]
            xn_rd = [None, None, None, None]
            ptr_ev = None
            ptr_ev_eng = "v"
            qkv_ev = [None] * FQKV
            qkv_mm = [None] * FQKV
            kvbf_done = [None] * H
            attn_ev = {}
            qdec_rd = [None, None]
            blkbuf_rd = [None, None]
            kv_upd = [None, None]
            gate_ev = [None, None]
            gated_done = [None] * KC
            yevac_st = [None, None]
            wblk_mm = [None, None, None]
            sq_rd = [None, None]
            qkd_done = [None, None]
            kdtm_done = [None, None]
            pa_free = [None, None, None]
            pa_slot = 0
            bcast_rd = None
            last_sq_mm = None
            s_rstd = None
            bcast_mm = None

            def pa_take():
                nonlocal pa_slot
                s = pa_slot
                pa_slot = (pa_slot + 1) % 3
                if pa_free[s] is not None:
                    P.wait("t", pa_free[s][0], pa_free[s][1])
                return s

            def emit_x_load(c, t4):
                par = t4 % 2
                row0 = c * SUB + t4 * 128
                P.wait("y", "v", xn_done[t4] if xn_done[t4] is not None else None)
                P.wait("y", "v", xn_done[(t4 + 2) % 4])
                x_ld[par] = P.emit("y", lambda e, row0=row0, par=par: e.dma_start(
                    x_sb[par][:], x_h[row0:row0 + 128, :]))

            def emit_S12_tile(c, t4):
                if True:
                    par = t4 % 2
                    P.wait("v", "y", x_ld[par])
                    P.wait("s", "y", x_ld[par])
                    b0 = par * 8
                    P.emit("v", lambda e, par=par, b0=b0: e.reduce_sum(
                        lnb[:, b0:b0 + 1], x_sb[par][:], axis=AX.X))
                    sv = P.emit("s", lambda e, par=par, b0=b0: e.activation(
                        xjunk[:], x_sb[par][:], AF.Square,
                        accum_out=lnb[:, b0 + 1:b0 + 2]))
                    P.drain("v")
                    P.emit("v", lambda e, b0=b0: e.tensor_scalar(
                        lnb[:, b0 + 2:b0 + 3], lnb[:, b0:b0 + 1], 1.0 / D, None, ALU.mult))
                    P.wait("v", "s", sv)
                    P.emit("v", lambda e, b0=b0: e.tensor_scalar(
                        lnb[:, b0 + 3:b0 + 4], lnb[:, b0 + 1:b0 + 2], 1.0 / D, None, ALU.mult))
                    P.drain("v")
                    P.emit("v", lambda e, b0=b0: e.tensor_scalar(
                        lnb[:, b0 + 4:b0 + 5], lnb[:, b0 + 2:b0 + 3],
                        lnb[:, b0 + 2:b0 + 3], None, ALU.mult))
                    P.drain("v")
                    vv = P.emit("v", lambda e, b0=b0: e.tensor_sub(
                        lnb[:, b0 + 5:b0 + 6], lnb[:, b0 + 3:b0 + 4], lnb[:, b0 + 4:b0 + 5]))
                    P.wait("s", "v", vv)
                    P.emit("s", lambda e, b0=b0: e.activation(
                        lnb[:, b0 + 6:b0 + 7], lnb[:, b0 + 5:b0 + 6], AF.Ln,
                        bias=misc[:, 0:1]))
                    P.drain("s")
                    sv2 = P.emit("s", lambda e, b0=b0: e.activation(
                        lnb[:, b0 + 7:b0 + 8], lnb[:, b0 + 6:b0 + 7], AF.Exp, scale=-0.5))
                    P.wait("v", "s", sv2)
                    if xn_rd[t4] is not None:
                        P.wait("v", "t", xn_rd[t4])
                    P.drain("v")
                    xn_done[t4] = P.emit("v", lambda e, t4=t4, par=par, b0=b0: e.tensor_scalar(
                        xn_sb[t4][:], x_sb[par][:], lnb[:, b0 + 2:b0 + 3],
                        lnb[:, b0 + 7:b0 + 8], ALU.subtract, ALU.mult))

            def emit_S3_unit(u):
                nonlocal ptr_ev, ptr_ev_eng, xnT_ev_last
                t4, wv = u // 2, u % 2
                P.wait("t", "v", xn_done[t4])
                if ptr_ev is not None:
                    P.wait("t", ptr_ev_eng, ptr_ev)
                trd = None
                for k in range(wv * 8, wv * 8 + 8):
                    trd = P.emit("t", lambda e, t4=t4, k=k, wv=wv: e.transpose(
                        ptr[:, (k - wv * 8) * 128:(k - wv * 8 + 1) * 128],
                        xn_sb[t4][:, k * 128:(k + 1) * 128], id_bf[:]))
                xn_rd[t4] = trd
                P.wait("v", "t", trd)
                ptr_ev = P.emit("v", lambda e, t4=t4, wv=wv: e.tensor_copy(
                    xnT[:].rearrange("p (k n) -> p k n", k=KC)[:, wv * 8:wv * 8 + 8,
                                                              t4 * 128:(t4 + 1) * 128],
                    ptr[:].rearrange("p (k n) -> p k n", k=8)))
                ptr_ev_eng = "v"
                if u == 7:
                    xnT_ev_last = ptr_ev

            def emit_S3_store(c):
                P.wait("y", "v", xnT_ev_last)
                P.emit("y", lambda e, c=c: e.dma_start(
                    xnc_h.rearrange("(k p) n -> p k n", p=128)[:, :, c * SUB:(c + 1) * SUB],
                    xnT[:].rearrange("p (k n) -> p k n", k=KC)))

            def emit_S3(c):
                for u in range(8):
                    emit_S3_unit(u)
                emit_S3_store(c)

            xnT_ev_last = None
            for t4 in range(2):
                emit_x_load(0, t4)
            for t4 in range(4):
                emit_S12_tile(0, t4)
                if t4 + 2 < 4:
                    emit_x_load(0, t4 + 2)
            emit_S3(0)

            atc_seq = []
            for _cc in range(NSUB):
                atc_seq += (list(range(H))
                            + [16 * (1 + g % 2) + g // 2 for g in range(2 * H)]
                            + [FQKV + m for m in range(KC)]
                            + [FQKV + KC + m for m in range(KC)])
            wld_done = [None] * len(atc_seq)
            n_wld = 0
            wld_use = 0

            def pump_wld():
                nonlocal n_wld
                if n_wld >= len(atc_seq):
                    return
                i = n_wld
                par = i % 3
                if wblk_mm[par] is not None:
                    P.wait("y", "t", wblk_mm[par])
                wld_done[i] = P.emit("y", lambda e, i=i, par=par: e.dma_start(
                    wblk[par][:], atc_h[atc_seq[i], :, :]))
                n_wld += 1

            for _ in range(3):
                pump_wld()

            for c in range(NSUB):
                def s4_group(f, moving, kdeps=None):
                    slot = pa_take()
                    P.wait("t", "y", wld_done[wld_use])
                    P.wait("t", "v", xnT_ev_last)
                    par = wld_use % 3
                    for k in range(KC):
                        if kdeps is not None:
                            P.wait("t", "v", kdeps[k])
                        wblk_mm[par] = P.emit("t", lambda e, par=par, slot=slot, k=k,
                                              moving=moving: e.matmul(
                            pa[:, slot * SUB:(slot + 1) * SUB],
                            wblk[par][:, k * 128:(k + 1) * 128],
                            moving[:, k * SUB:(k + 1) * SUB],
                            start=(k == 0), stop=(k == KC - 1)))
                    return slot, wblk_mm[par]

                def s4_qkv(f):
                    slot, mm = s4_group(f, xnT)
                    qkv_mm[f] = mm
                    P.wait("s", "t", mm)
                    ev = P.emit("s", lambda e, f=f, slot=slot: e.activation(
                        qkvT[:, f * SUB:(f + 1) * SUB], pa[:, slot * SUB:(slot + 1) * SUB],
                        AF.Silu))
                    qkv_ev[f] = ev
                    pa_free[slot] = ("s", ev)

                def bump_wld():
                    nonlocal wld_use
                    wld_use += 1
                    pump_wld()

                stash = {}

                def emit_A(b, h):
                    B = 2 * c + b
                    par = b
                    qoff = h * SUB + b * BLK
                    koff = (16 + h) * SUB + b * BLK
                    voff = (32 + h) * SUB + b * BLK
                    nonlocal ptr_ev, ptr_ev_eng
                    P.wait("v", "s", qkv_ev[32 + h])
                    P.wait("t", "s", qkv_ev[32 + h])
                    if qdec_rd[par] is not None:
                        P.wait("v", "t", qdec_rd[par])
                    vq = P.emit("v", lambda e, par=par, qoff=qoff, h=h: e.tensor_tensor(
                        qdec[par][:], qkvT[:, qoff:qoff + BLK],
                        qdecay[:, h * BLK:(h + 1) * BLK], ALU.mult))
                    if ptr_ev is not None:
                        P.wait("t", ptr_ev_eng, ptr_ev)
                    trd = None
                    for j in range(2):
                        trd = P.emit("t", lambda e, koff=koff, j=j: e.transpose(
                            ptr[:, j * 128:(j + 1) * 128],
                            qkvT[:, koff + j * 128:koff + (j + 1) * 128], id_bf[:]))
                    for j in range(2):
                        trd = P.emit("t", lambda e, voff=voff, j=j: e.transpose(
                            ptr[:, BLK + j * 128:BLK + (j + 1) * 128],
                            qkvT[:, voff + j * 128:voff + (j + 1) * 128], id_bf[:]))
                    P.wait("s", "t", trd)
                    if blkbuf_rd[par] is not None:
                        P.wait("s", "t", blkbuf_rd[par])
                    P.emit("s", lambda e, par=par: e.activation(
                        ktm[par][:], ptr[:, 0:BLK], AF.Identity))
                    ptr_ev = P.emit("s", lambda e, par=par: e.activation(
                        vtm[par][:], ptr[:, BLK:2 * BLK], AF.Identity))
                    ptr_ev_eng = "s"
                    P.wait("v", "s", ptr_ev)
                    for j in range(2):
                        kdtm_done[par] = P.emit("v", lambda e, par=par, j=j, h=h: e.tensor_scalar(
                            kdtm[par][:, j * 128:(j + 1) * 128],
                            ktm[par][:, j * 128:(j + 1) * 128],
                            kdecay[:, 2 * h + j:2 * h + j + 1], None, ALU.mult))
                    slot = pa_take()
                    if qkd_done[par] is not None:
                        P.wait("t", "v", qkd_done[par])
                    qkmm = None
                    for j in range(2):
                        qkmm = P.emit("t", lambda e, slot=slot, j=j, koff=koff, qoff=qoff: e.matmul(
                            pa[:, slot * SUB + j * BLK:slot * SUB + (j + 1) * BLK],
                            qkvT[:, koff + j * 128:koff + (j + 1) * 128],
                            qkvT[:, qoff:qoff + BLK], start=True, stop=True))
                    P.wait("v", "t", qkmm)
                    qkd_done[par] = P.emit("v", lambda e, slot=slot, par=par, h=h: e.tensor_tensor(
                        qkd[par][:], pa[:, slot * SUB:(slot + 1) * SUB],
                        diag[:, h * 2 * BLK:(h + 1) * 2 * BLK], ALU.mult))
                    pa_free[slot] = ("v", qkd_done[par])
                    stash[(b, h)] = (par, vq, B, ptr_ev)

                def emit_C(b, h):
                    par, vq, B, pev = stash.pop((b, h))
                    P.wait("t", "v", vq)
                    P.wait("t", "v", qkd_done[par])
                    P.wait("t", "s", pev)
                    if attn_ev.get(par) is not None:
                        P.wait("t", "s", attn_ev[par])
                    amm = None
                    for j in range(2):
                        amm = P.emit("t", lambda e, par=par, j=j: e.matmul(
                            pattn[:, par * SUB:par * SUB + BLK],
                            vtm[par][:, j * 128:(j + 1) * 128],
                            qkd[par][:, j * BLK:(j + 1) * BLK],
                            start=(j == 0), stop=(B == 0 and j == 1)))
                    if B > 0:
                        P.wait("t", "s", kvbf_done[h])
                        amm = P.emit("t", lambda e, par=par, h=h: e.matmul(
                            pattn[:, par * SUB:par * SUB + BLK],
                            kvbf[:, h * DH:(h + 1) * DH], qdec[par][:],
                            start=False, stop=True))
                    qdec_rd[par] = amm
                    P.wait("t", "v", kdtm_done[par])
                    if kv_upd[par] is not None:
                        P.wait("t", "s", kv_upd[par])
                    kmm = None
                    for j in range(2):
                        kmm = P.emit("t", lambda e, par=par, j=j: e.matmul(
                            pkvu[:, par * 128:(par + 1) * 128],
                            kdtm[par][:, j * 128:(j + 1) * 128],
                            vtm[par][:, j * 128:(j + 1) * 128],
                            start=(j == 0), stop=(j == 1)))
                    blkbuf_rd[par] = kmm
                    P.wait("v", "t", kmm)
                    if B == 0:
                        P.emit("v", lambda e, par=par, h=h: e.tensor_copy(
                            kv_sb[:, h * DH:(h + 1) * DH],
                            pkvu[:, par * 128:(par + 1) * 128]))
                    else:
                        P.emit("v", lambda e, par=par, h=h: e.scalar_tensor_tensor(
                            kv_sb[:, h * DH:(h + 1) * DH], kv_sb[:, h * DH:(h + 1) * DH],
                            bd_sb[:, h:h + 1], pkvu[:, par * 128:(par + 1) * 128],
                            ALU.mult, ALU.add))
                    kvs = P.cnt["v"]
                    P.wait("s", "v", kvs)
                    kv_upd[par] = P.emit("s", lambda e, h=h: e.activation(
                        kvbf[:, h * DH:(h + 1) * DH], kv_sb[:, h * DH:(h + 1) * DH],
                        AF.Identity))
                    kvbf_done[h] = kv_upd[par]
                    P.wait("s", "t", amm)
                    attn_ev[par] = P.emit("s", lambda e, par=par, h=h, b=b: e.activation(
                        attnT[:, h * SUB + b * BLK:h * SUB + (b + 1) * BLK],
                        pattn[:, par * SUB:par * SUB + BLK], AF.Identity))

                def emit_sq(h):
                    nonlocal last_sq_mm
                    par = h % 2
                    if sq_rd[par] is not None:
                        P.wait("s", "t", sq_rd[par])
                    ssq = P.emit("s", lambda e, h=h, par=par: e.activation(
                        sq[par][:], attnT[:, h * SUB:(h + 1) * SUB], AF.Square))
                    P.wait("t", "s", ssq)
                    if h == 0 and bcast_rd is not None:
                        P.wait("t", "v", bcast_rd)
                    sq_rd[par] = P.emit("t", lambda e, h=h, par=par: e.matmul(
                        pmisc[0:1, 0:SUB], ones_bf[:], sq[par][:],
                        start=(h == 0), stop=(h == H - 1)))
                    last_sq_mm = sq_rd[par]

                for f in range(H):
                    s4_qkv(f)
                    bump_wld()
                    if c + 1 < NSUB:
                        if f == 0:
                            emit_x_load(c + 1, 0)
                        elif f == 1:
                            emit_x_load(c + 1, 1)
                        elif f == 3:
                            emit_S12_tile(c + 1, 0)
                        elif f == 5:
                            emit_x_load(c + 1, 2)
                        elif f == 7:
                            emit_S12_tile(c + 1, 1)
                        elif f == 9:
                            emit_x_load(c + 1, 3)
                        elif f == 11:
                            emit_S12_tile(c + 1, 2)
                        elif f == 13:
                            emit_S12_tile(c + 1, 3)

                for g in range(32 + 3):
                    if g < 32:
                        h = g // 2
                        f = (16 + h) if g % 2 == 0 else (32 + h)
                        s4_qkv(f)
                        bump_wld()
                    if 3 <= g < 32 + 3:
                        i = g - 3
                        emit_C(i % 2, i // 2)
                        if i % 2 == 1:
                            emit_sq(i // 2)
                    if 2 <= g < 32 + 2:
                        i = g - 2
                        emit_A(i % 2, i // 2)

                P.wait("s", "t", last_sq_mm)
                P.emit("s", lambda e: e.activation(
                    lnrow[0:1, 0:SUB], pmisc[0:1, 0:SUB], AF.Ln,
                    scale=1.0 / D, bias=misc[0:1, 1:2]))
                P.drain("s")
                s_rstd = P.emit("s", lambda e: e.activation(
                    lnrow[0:1, SUB:2 * SUB], lnrow[0:1, 0:SUB], AF.Exp, scale=-0.5))

                for m in range(KC):
                    slot, mm = s4_group(None, xnT)
                    if m == 0:
                        P.wait("t", "s", s_rstd)
                        bcast_mm = P.emit("t", lambda e: e.matmul(
                            pmisc[:, 0:SUB], ones_f32[0:1, :], lnrow[0:1, SUB:2 * SUB],
                            start=True, stop=True))
                    bump_wld()
                    ppar = m % 2
                    P.wait("s", "t", mm)
                    if gate_ev[ppar] is not None:
                        P.wait("s", "v", gated_done[m - 2] if m >= 2 else None)
                    ev = P.emit("s", lambda e, slot=slot, ppar=ppar: e.activation(
                        gate[ppar][:], pa[:, slot * SUB:(slot + 1) * SUB], AF.Sigmoid))
                    gate_ev[ppar] = ev
                    pa_free[slot] = ("s", ev)
                    P.wait("v", "t", bcast_mm)
                    bcast_rd = P.emit("v", lambda e, m=m, ppar=ppar: e.scalar_tensor_tensor(
                        gtmp[ppar][:], attnT[:, m * SUB:(m + 1) * SUB],
                        normw_sb[:, m:m + 1], pmisc[:, 0:SUB], ALU.mult, ALU.mult))
                    P.wait("v", "s", gate_ev[ppar])
                    P.drain("v")
                    gated_done[m] = P.emit("v", lambda e, m=m, ppar=ppar: e.tensor_tensor(
                        gated[:, m * SUB:(m + 1) * SUB], gtmp[ppar][:], gate[ppar][:], ALU.mult))

                for m in range(KC):
                    ppar = m % 2
                    slot, mm = s4_group(None, gated,
                                        kdeps=gated_done if m == 0 else None)
                    bump_wld()
                    P.wait("s", "t", mm)
                    if yevac_st[ppar] is not None:
                        P.wait("s", "g", yevac_st[ppar])
                    ev = P.emit("s", lambda e, slot=slot, ppar=ppar: e.activation(
                        yevac[ppar][:], pa[:, slot * SUB:(slot + 1) * SUB], AF.Identity))
                    pa_free[slot] = ("s", ev)
                    P.wait("g", "s", ev)
                    yevac_st[ppar] = P.emit("g", lambda e, m=m, c=c, ppar=ppar: e.dma_start(
                        ysc_h[m * 128:(m + 1) * 128, c * SUB:(c + 1) * SUB], yevac[ppar][:]))
                    if c + 1 < NSUB and m % 2 == 0:
                        emit_S3_unit(m // 2)
                if c + 1 < NSUB:
                    emit_S3_store(c + 1)

                for _ in range(20):
                    if wd_idx >= len(wd_items):
                        break
                    wsrc, p10w, mw = wd_items[wd_idx]
                    wd_st[wd_idx] = P.emit("g", lambda e, wsrc=wsrc, p10w=p10w, mw=mw: e.dma_start(
                        wdc_h[p10w, mw, :, :].rearrange("p (j c) -> p j c", j=HJ), wsrc))
                    wd_idx += 1

            P.barrier()

        with ExitStack() as SB:
            eb = SB.enter_context
            xnTg = eb(nc.sbuf_tensor("xnTg", [128, KC * G], BF16))
            acc = eb(nc.sbuf_tensor("acc", [128, KC * G], F32))
            gu = eb(nc.sbuf_tensor("gu", [128, HJ * G], BF16))
            wgbuf = [eb(nc.sbuf_tensor(f"wgbuf{i}", [128, D], BF16)) for i in range(2)]
            wubuf = [eb(nc.sbuf_tensor(f"wubuf{i}", [128, D], BF16)) for i in range(2)]
            wdbuf = [eb(nc.sbuf_tensor(f"wdbuf{i}", [128, HEXP], BF16)) for i in range(2)]
            gsil = [eb(nc.sbuf_tensor(f"gsil{i}", [128, G], BF16)) for i in range(2)]
            ce_sb = eb(nc.sbuf_tensor("ce_sb", [128, G], F32))
            ce_row = [eb(nc.sbuf_tensor(f"ce_row{i}", [1, G], F32)) for i in range(2)]
            ceT = eb(nc.sbuf_tensor("ceT", [8, G], F32))
            le = eb(nc.sbuf_tensor("le", [8, G], F32))
            gbig = eb(nc.sbuf_tensor("gbig", [128, 512], F32))
            rb = [eb(nc.sbuf_tensor(f"rb{i}", [128, G], F32)) for i in range(2)]
            xload = eb(nc.sbuf_tensor("xload", [128, D], F32))
            outst = [eb(nc.sbuf_tensor(f"outst{i}", [128, D], F32)) for i in range(2)]

            pg = eb(nc.psum_tensor("pg", [128, G], F32))
            pu = eb(nc.psum_tensor("pu", [128, G], F32))
            pra = eb(nc.psum_tensor("pra", [128, G], F32))
            prb = eb(nc.psum_tensor("prb", [128, G], F32))

            GT, GE, GN, GQ, GM, GK, GC = 0, 64, 128, 192, 256, 320, 384
            SNM, SSM, SRC, SV1, SV2, SDN, SR2 = 448, 456, 464, 472, 480, 488, 496

            xnTg_rd = None
            gu_rd = None
            acc_rd = None
            dp_done = [None, None]
            acc_add = [None] * KC
            outst_st = [None, None]
            rb_rd = [None, None]
            gmm_done = [None, None]
            umm_done = [None, None]
            gsil_rd = [None, None]
            ssil_last = None
            wd_rd = [None, None]
            ce_done = None
            cbm_prev = None
            fin_tr = None
            evac_s = [None, None]
            evac_v = [None, None]
            dp_v = [None, None]
            gu_ld = {}
            wd_ld2 = {}
            ce_emitted = set()

            for gi in range(NG):
                if xnTg_rd is not None:
                    P.wait("g", "t", xnTg_rd)
                xl = P.emit("g", lambda e, gi=gi: e.dma_start(
                    xnTg[:].rearrange("p (k n) -> p k n", k=KC),
                    xnc_h.rearrange("(k p) n -> p k n", p=128)[:, :, gi * G:(gi + 1) * G]))
                P.wait("t", "g", xl)
                if fin_tr is not None:
                    P.wait("t", "v", acc_rd)
                rmm = None
                for k in range(KC):
                    for q in range(2):
                        rmm = P.emit("t", lambda e, k=k, q=q: e.matmul(
                            pra[0:8, q * 512:(q + 1) * 512],
                            rwT[:, k * 8:(k + 1) * 8],
                            xnTg[:, k * G + q * 512:k * G + (q + 1) * 512],
                            start=(k == 0), stop=(k == KC - 1)))
                P.wait("v", "t", rmm)
                lev = P.emit("v", lambda e: e.tensor_copy(le[:], pra[0:8, :]))
                P.wait("t", "v", lev)
                ttr = None
                for tc in range(8):
                    ttr = P.emit("t", lambda e, tc=tc: e.transpose(
                        pu[:, tc * 8:(tc + 1) * 8], le[0:8, tc * 128:(tc + 1) * 128],
                        id_f32[0:8, 0:8]))
                P.wait("v", "t", ttr)
                P.emit("v", lambda e: e.tensor_copy(gbig[:, GT:GT + 64], pu[:, 0:64]))
                P.drain("v")
                vstep = None
                for tc in range(8):
                    vstep = P.emit("v", lambda e, tc=tc: e.reduce_max(
                        gbig[:, SNM + tc:SNM + tc + 1], gbig[:, GT + tc * 8:GT + (tc + 1) * 8],
                        axis=AX.X, negate=True))
                P.wait("s", "v", vstep)
                sexp = None
                for tc in range(8):
                    sexp = P.emit("s", lambda e, tc=tc: e.activation(
                        gbig[:, GE + tc * 8:GE + (tc + 1) * 8],
                        gbig[:, GT + tc * 8:GT + (tc + 1) * 8], AF.Exp,
                        bias=gbig[:, SNM + tc:SNM + tc + 1]))
                P.wait("v", "s", sexp)
                for tc in range(8):
                    P.emit("v", lambda e, tc=tc: e.reduce_sum(
                        gbig[:, SSM + tc:SSM + tc + 1], gbig[:, GE + tc * 8:GE + (tc + 1) * 8],
                        axis=AX.X))
                P.drain("v")
                P.emit("v", lambda e: e.reciprocal(gbig[:, SRC:SRC + 8], gbig[:, SSM:SSM + 8]))
                P.drain("v")
                for tc in range(8):
                    P.emit("v", lambda e, tc=tc: e.tensor_scalar(
                        gbig[:, GN + tc * 8:GN + (tc + 1) * 8],
                        gbig[:, GE + tc * 8:GE + (tc + 1) * 8],
                        gbig[:, SRC + tc:SRC + tc + 1], None, ALU.mult))
                P.drain("v")
                for tc in range(8):
                    P.emit("v", lambda e, tc=tc: e.reduce_max(
                        gbig[:, SV1 + tc:SV1 + tc + 1], gbig[:, GN + tc * 8:GN + (tc + 1) * 8],
                        axis=AX.X))
                P.drain("v")
                for tc in range(8):
                    P.emit("v", lambda e, tc=tc: e.tensor_scalar(
                        gbig[:, GQ + tc * 8:GQ + (tc + 1) * 8],
                        gbig[:, GN + tc * 8:GN + (tc + 1) * 8],
                        gbig[:, SV1 + tc:SV1 + tc + 1], None, ALU.is_ge))
                P.drain("v")
                for tc in range(8):
                    P.emit("v", lambda e, tc=tc: e.scalar_tensor_tensor(
                        gbig[:, GM + tc * 8:GM + (tc + 1) * 8],
                        gbig[:, GQ + tc * 8:GQ + (tc + 1) * 8], -2.0,
                        gbig[:, GN + tc * 8:GN + (tc + 1) * 8], ALU.mult, ALU.add))
                P.drain("v")
                for tc in range(8):
                    P.emit("v", lambda e, tc=tc: e.reduce_max(
                        gbig[:, SV2 + tc:SV2 + tc + 1], gbig[:, GM + tc * 8:GM + (tc + 1) * 8],
                        axis=AX.X))
                P.drain("v")
                P.emit("v", lambda e: e.tensor_add(
                    gbig[:, SDN:SDN + 8], gbig[:, SV1:SV1 + 8], gbig[:, SV2:SV2 + 8]))
                P.drain("v")
                P.emit("v", lambda e: e.tensor_scalar(
                    gbig[:, SDN:SDN + 8], gbig[:, SDN:SDN + 8], 1e-20, None, ALU.add))
                P.drain("v")
                P.emit("v", lambda e: e.reciprocal(gbig[:, SR2:SR2 + 8], gbig[:, SDN:SDN + 8]))
                P.drain("v")
                for tc in range(8):
                    P.emit("v", lambda e, tc=tc: e.tensor_scalar(
                        gbig[:, GK + tc * 8:GK + (tc + 1) * 8],
                        gbig[:, GN + tc * 8:GN + (tc + 1) * 8],
                        gbig[:, SV2 + tc:SV2 + tc + 1], None, ALU.is_ge))
                P.drain("v")
                vcoef = None
                for tc in range(8):
                    vcoef = P.emit("v", lambda e, tc=tc: e.scalar_tensor_tensor(
                        gbig[:, GC + tc * 8:GC + (tc + 1) * 8],
                        gbig[:, GN + tc * 8:GN + (tc + 1) * 8],
                        gbig[:, SR2 + tc:SR2 + tc + 1],
                        gbig[:, GK + tc * 8:GK + (tc + 1) * 8], ALU.mult, ALU.mult))
                P.wait("t", "v", vcoef)
                ctr = None
                for tc in range(8):
                    ctr = P.emit("t", lambda e, tc=tc: e.transpose(
                        pra[0:8, tc * 128:(tc + 1) * 128],
                        gbig[:, GC + tc * 8:GC + (tc + 1) * 8], id_f32[:]))
                P.wait("v", "t", ctr)
                cev = P.emit("v", lambda e: e.tensor_copy(ceT[:], pra[0:8, :]))

                passlist = [8, 9] + list(range(NE))

                def emit_gu_load(p10, j):
                    par = j % 2
                    if p10 < NE:
                        src_g = wg_h[p10, :, j * 128:(j + 1) * 128].rearrange(
                            "(k p) c -> p k c", p=128)
                        src_u = wu_h[p10, :, j * 128:(j + 1) * 128].rearrange(
                            "(k p) c -> p k c", p=128)
                    else:
                        half = p10 - NE
                        off = half * HEXP + j * 128
                        src_g = sg_h[:, off:off + 128].rearrange("(k p) c -> p k c", p=128)
                        src_u = su_h[:, off:off + 128].rearrange("(k p) c -> p k c", p=128)
                    if gmm_done[par] is not None:
                        P.wait("g", "t", gmm_done[par])
                        P.wait("g", "t", umm_done[par])
                    P.emit("g", lambda e, src_g=src_g, par=par: e.dma_start(
                        wgbuf[par][:].rearrange("p (k c) -> p k c", k=KC), src_g))
                    gu_ld[(p10, j)] = P.emit("g", lambda e, src_u=src_u, par=par: e.dma_start(
                        wubuf[par][:].rearrange("p (k c) -> p k c", k=KC), src_u))

                def emit_wd_load(p10, m):
                    par = m % 2
                    if wd_rd[par] is not None:
                        P.wait("y", "t", wd_rd[par])
                    wd_ld2[(p10, m)] = P.emit("y", lambda e, p10=p10, m=m, par=par: e.dma_start(
                        wdbuf[par][:], wdc_h[p10, m, :, :]))

                def emit_ce(p10):
                    nonlocal cbm_prev, ce_done
                    cpar = p10 % 2
                    P.wait("g", "v", cev)
                    if ce_done is not None:
                        P.wait("g", "t", cbm_prev)
                    crl = P.emit("g", lambda e, p10=p10, cpar=cpar: e.dma_start(
                        ce_row[cpar][:], ceT[p10:p10 + 1, :]))
                    P.wait("t", "g", crl)
                    P.wait("t", "v", cev)
                    if evac_s[0] is not None:
                        P.wait("t", "s", evac_s[0])
                    cbm = None
                    for q in range(2):
                        cbm = P.emit("t", lambda e, cpar=cpar, q=q: e.matmul(
                            pg[:, q * 512:(q + 1) * 512], ones_f32[0:1, :],
                            ce_row[cpar][0:1, q * 512:(q + 1) * 512],
                            start=True, stop=True))
                    cbm_prev = cbm
                    P.wait("v", "t", cbm)
                    ce_done = P.emit("v", lambda e: e.tensor_copy(ce_sb[:], pg[:]))

                for pi, p10 in enumerate(passlist):
                    first_pass = (pi == 0)
                    nxt_p10 = passlist[pi + 1] if pi + 1 < len(passlist) else (
                        passlist[0] if gi + 1 < NG else None)
                    if p10 < NE and (p10 not in ce_emitted):
                        emit_ce(p10)
                    ce_emitted.discard(p10)
                    for j in range(HJ):
                        par = j % 2
                        rg, ru = (pg, pu) if par == 0 else (pra, prb)
                        if (p10, j) not in gu_ld:
                            emit_gu_load(p10, j)
                        lu = gu_ld.pop((p10, j))
                        P.wait("t", "g", lu)
                        if evac_s[par] is not None:
                            P.wait("t", "s", evac_s[par])
                        if evac_v[par] is not None:
                            P.wait("t", "v", evac_v[par])
                        if dp_v[par] is not None:
                            P.wait("t", "v", dp_v[par])
                        if p10 < NE and j == 0:
                            P.wait("t", "v", ce_done)
                        for q in range(2):
                            for k in range(KC):
                                gmm_done[par] = P.emit("t", lambda e, rg=rg, par=par, k=k, q=q: e.matmul(
                                    rg[:, q * 512:(q + 1) * 512],
                                    wgbuf[par][:, k * 128:(k + 1) * 128],
                                    xnTg[:, k * G + q * 512:k * G + (q + 1) * 512],
                                    start=(k == 0), stop=(k == KC - 1)))
                        for q in range(2):
                            for k in range(KC):
                                umm_done[par] = P.emit("t", lambda e, ru=ru, par=par, k=k, q=q: e.matmul(
                                    ru[:, q * 512:(q + 1) * 512],
                                    wubuf[par][:, k * 128:(k + 1) * 128],
                                    xnTg[:, k * G + q * 512:k * G + (q + 1) * 512],
                                    start=(k == 0), stop=(k == KC - 1)))
                        P.wait("s", "t", gmm_done[par])
                        if gsil_rd[par] is not None:
                            P.wait("s", "v", gsil_rd[par])
                        ssil = P.emit("s", lambda e, rg=rg, par=par: e.activation(
                            gsil[par][:], rg[:], AF.Silu))
                        ssil_last = ssil
                        evac_s[par] = ssil
                        P.wait("v", "s", ssil)
                        P.wait("v", "t", umm_done[par])
                        vgu = P.emit("v", lambda e, ru=ru, par=par, j=j: e.tensor_tensor(
                            gu[:, j * G:(j + 1) * G], gsil[par][:], ru[:], ALU.mult))
                        gsil_rd[par] = vgu
                        if p10 < NE:
                            P.wait("v", "v", ce_done)
                            P.drain("v")
                            vgu = P.emit("v", lambda e, j=j: e.tensor_tensor(
                                gu[:, j * G:(j + 1) * G], gu[:, j * G:(j + 1) * G],
                                ce_sb[:], ALU.mult))
                        evac_v[par] = vgu
                        gu_rd = vgu
                        if j == HJ - 1:
                            gu_all = vgu
                        if j == 9:
                            emit_wd_load(p10, 0)
                        elif j == 10:
                            emit_wd_load(p10, 1)
                    xnTg_rd = umm_done[(HJ - 1) % 2]
                    for m in range(KC):
                        par = m % 2
                        rdp = pra if par == 0 else prb
                        if (p10, m) not in wd_ld2:
                            emit_wd_load(p10, m)
                        wld = wd_ld2.pop((p10, m))
                        P.wait("t", "y", wld)
                        P.wait("t", "v", gu_all)
                        P.wait("t", "s", ssil_last)
                        if dp_done[par] is not None:
                            P.wait("t", "v", dp_done[par])
                        dmm = None
                        for q in range(2):
                            for j in range(HJ):
                                dmm = P.emit("t", lambda e, rdp=rdp, par=par, j=j, q=q: e.matmul(
                                    rdp[:, q * 512:(q + 1) * 512],
                                    wdbuf[par][:, j * 128:(j + 1) * 128],
                                    gu[:, j * G + q * 512:j * G + (q + 1) * 512],
                                    start=(j == 0), stop=(j == HJ - 1)))
                        wd_rd[par] = dmm
                        if m + 2 < KC:
                            emit_wd_load(p10, m + 2)
                        if nxt_p10 is not None:
                            if m == 2:
                                emit_gu_load(nxt_p10, 0)
                            elif m == 5:
                                emit_gu_load(nxt_p10, 1)
                            elif m == 3 and nxt_p10 < NE:
                                emit_ce(nxt_p10)
                                ce_emitted.add(nxt_p10)
                        P.wait("v", "t", dmm)
                        if first_pass:
                            dp_done[par] = P.emit("v", lambda e, rdp=rdp, m=m: e.tensor_copy(
                                acc[:, m * G:(m + 1) * G], rdp[:]))
                        else:
                            dp_done[par] = P.emit("v", lambda e, rdp=rdp, m=m: e.tensor_tensor(
                                acc[:, m * G:(m + 1) * G], acc[:, m * G:(m + 1) * G],
                                rdp[:], ALU.add))
                        dp_v[1] = dp_done[par]
                        acc_add[m] = dp_done[par]

                for m in range(KC):
                    par = m % 2
                    if rb_rd[par] is not None:
                        P.wait("g", "v", rb_rd[par])
                    rl = P.emit("g", lambda e, m=m, gi=gi, par=par: e.dma_start(
                        rb[par][:], ysc_h[m * 128:(m + 1) * 128, gi * G:(gi + 1) * G]))
                    P.wait("v", "g", rl)
                    P.wait("v", "v", acc_add[m])
                    P.drain("v")
                    rb_rd[par] = P.emit("v", lambda e, m=m, par=par: e.tensor_tensor(
                        acc[:, m * G:(m + 1) * G], acc[:, m * G:(m + 1) * G],
                        rb[par][:], ALU.add))
                    acc_add[m] = rb_rd[par]
                fin_regions = [pg[:, 0:512], pg[:, 512:1024], pu[:, 0:512], pu[:, 512:1024]]
                for t8 in range(8):
                    spar = t8 % 2
                    P.wait("t", "v", acc_add[KC - 1])
                    if ssil_last is not None:
                        P.wait("t", "s", ssil_last)
                    if acc_rd is not None:
                        P.wait("t", "v", acc_rd)
                    ftr = None
                    for m in range(KC):
                        ftr = P.emit("t", lambda e, m=m, t8=t8: e.transpose(
                            fin_regions[m // 4][:, (m % 4) * 128:(m % 4 + 1) * 128],
                            acc[:, m * G + t8 * 128:m * G + (t8 + 1) * 128], id_f32[:]))
                    fin_tr = ftr
                    row0 = gi * G + t8 * 128
                    if acc_rd is not None:
                        P.wait("y", "v", acc_rd)
                    xl2 = P.emit("y", lambda e, row0=row0: e.dma_start(
                        xload[:], x_h[row0:row0 + 128, :]))
                    P.wait("v", "t", ftr)
                    P.wait("v", "y", xl2)
                    if outst_st[spar] is not None:
                        P.wait("v", "y", outst_st[spar])
                    vo = None
                    for q in range(4):
                        vo = P.emit("v", lambda e, q=q, spar=spar: e.tensor_tensor(
                            outst[spar][:, q * 512:(q + 1) * 512], fin_regions[q],
                            xload[:, q * 512:(q + 1) * 512], ALU.add))
                    acc_rd = vo
                    P.wait("y", "v", vo)
                    outst_st[spar] = P.emit("y", lambda e, row0=row0, spar=spar: e.dma_start(
                        out_h[row0:row0 + 128, :], outst[spar][:]))

            P.wait("y", "y", P.cnt["y"])
            P.wait("g", "g", P.cnt["g"])

        with nc.Block() as blk:
            @blk.tensor
            def _(t):
                for fn in P.ops["t"]:
                    fn(t)

            @blk.vector
            def _(v):
                for fn in P.ops["v"]:
                    fn(v)

            @blk.scalar
            def _(s):
                for fn in P.ops["s"]:
                    fn(s)

            @blk.gpsimd
            def _(g):
                for fn in P.ops["g"]:
                    fn(g)

            @blk.sync
            def _(y):
                for fn in P.ops["y"]:
                    fn(y)

    if return_plan:
        return nc, P
    return nc


def _consts():
    bf = ml_dtypes.bfloat16
    c = {}
    c["c_id_bf"] = np.eye(128, dtype=np.float32).astype(bf)
    c["c_id_f32"] = np.eye(128, dtype=np.float32)
    c["c_ones_f32"] = np.ones((128, 128), dtype=np.float32)
    c["c_ones_bf"] = np.ones((128, 1), dtype=np.float32).astype(bf)
    ar = np.arange(1, BLK + 1, dtype=np.float32)
    c["c_ar"] = np.broadcast_to(ar, (128, BLK)).copy()
    rev = np.zeros((128, 2), dtype=np.float32)
    for j in range(2):
        rev[:, j] = 255.0 - (j * 128 + np.arange(128))
    c["c_revar"] = rev
    idx = np.zeros((128, 2 * BLK), dtype=np.float32)
    for kc in range(2):
        kk = kc * 128 + np.arange(128)[:, None]
        mm = np.arange(BLK)[None, :]
        dv = mm - kk
        idx[:, kc * BLK:(kc + 1) * BLK] = np.where(dv >= 0, dv, 1e6).astype(np.float32)
    c["c_idx"] = idx
    mi = np.zeros((128, 8), dtype=np.float32)
    mi[:, 0] = 1e-5
    mi[:, 1] = 1e-6
    mi[:, 2] = 1e-20
    c["c_misc"] = mi
    return c


def kernel(L0_attn_hidden_states, L0_attn_slope_rate, L0_attn_w_qkv, L0_attn_w_gate,
           L0_attn_w_out, L0_attn_norm_w, L0_ffn_router_w, L0_ffn_wg, L0_ffn_wu,
           L0_ffn_wd, L0_ffn_sg, L0_ffn_su, L0_ffn_sd):
    hid = np.ascontiguousarray(np.asarray(L0_attn_hidden_states, dtype=np.float32))
    slope = np.asarray(L0_attn_slope_rate, dtype=np.float32).reshape(1, H)
    normw = np.asarray(L0_attn_norm_w, dtype=np.float32)
    normw_t = np.ascontiguousarray(normw.reshape(KC, 128).T)

    shared = {
        "w_qkv": np.ascontiguousarray(np.asarray(L0_attn_w_qkv, dtype=np.float32)),
        "w_gate": np.ascontiguousarray(np.asarray(L0_attn_w_gate, dtype=np.float32)),
        "w_out": np.ascontiguousarray(np.asarray(L0_attn_w_out, dtype=np.float32)),
        "slope": slope,
        "router_w": np.ascontiguousarray(np.asarray(L0_ffn_router_w, dtype=np.float32)),
        "wg": np.ascontiguousarray(np.asarray(L0_ffn_wg, dtype=np.float32)),
        "wu": np.ascontiguousarray(np.asarray(L0_ffn_wu, dtype=np.float32)),
        "wd": np.ascontiguousarray(np.asarray(L0_ffn_wd, dtype=np.float32)),
        "sg": np.ascontiguousarray(np.asarray(L0_ffn_sg, dtype=np.float32)),
        "su": np.ascontiguousarray(np.asarray(L0_ffn_su, dtype=np.float32)),
        "sd": np.ascontiguousarray(np.asarray(L0_ffn_sd, dtype=np.float32)),
        "c_normw_t": normw_t,
    }
    shared.update(_consts())

    nc = build_program()
    in_maps = []
    for c in range(8):
        m = dict(shared)
        m["x"] = np.ascontiguousarray(hid[c])
        in_maps.append(m)
    res = run_bass_kernel_spmd(nc, in_maps, core_ids=list(range(8)))
    return np.stack([r["out"] for r in res.results], axis=0)


def simulate_plan(P):
    pos = {e: 0 for e in ENGS}
    sem = {e: 0 for e in ENGS}
    progress = True
    while progress:
        progress = False
        for e in ENGS:
            log = P.log[e]
            while pos[e] < len(log):
                ent = log[pos[e]]
                if ent[0] == "wait":
                    if sem[ent[1]] < ent[2]:
                        break
                elif ent[0] == "inc":
                    sem[e] += ent[1]
                pos[e] += 1
                progress = True
    stuck = {e: (pos[e], len(P.log[e])) for e in ENGS if pos[e] < len(P.log[e])}
    if stuck:
        print("DEADLOCK; stuck engines:", stuck)
        for e, (p, n) in stuck.items():
            print(f"  {e} at {p}/{n}: {P.log[e][p]}  sems={sem}")
        return False
    print("schedule simulation: no deadlock")
    return True
